# version 04 (final)
# baseline (speedup 1.0000x reference)
import os
import numpy as np
import ml_dtypes
from contextlib import ExitStack
import concourse.bass as bass
import concourse.mybir as mybir
from concourse.bass_utils import run_bass_kernel_spmd

F32 = mybir.dt.float32
BF16 = mybir.dt.bfloat16
AF = mybir.ActivationFunctionType
ALU = mybir.AluOpType
AX = mybir.AxisListType.X

NCORES = 8
B, T, D = 2, 8192, 1024
TC = 2048
NDBLK = D // 128
NST = 256
DFF = 2736
NP_FF = (DFF + 127) // 128
DIN, DST, DTR, DCONV = 2048, 16, 64, 4
NDIN = DIN // 128
E, EI, SHI = 8, 512, 1024
MOE_SCALE = 2.5
FC = 512
NT = TC // FC
HF = 1024
BIG = 1.0e6
LN_EPS = 1e-5
RMS_EPS = 1e-6

_UPTO = int(os.environ.get("MK_UPTO", "4"))
_DEBUG = os.environ.get("MK_DEBUG", "")


class _DumpDone(Exception):
    pass

bf = ml_dtypes.bfloat16


class Sem:
    def __init__(self, ctx, nc, name):
        self.h = ctx.enter_context(nc.semaphore(name))
        self.v = 0

    def inc(self, instr, n=1):
        instr.then_inc(self.h, n)
        self.v += n

    def wait(self, eng, v=None):
        eng.wait_ge(self.h, self.v if v is None else v)


def build_program():
    nc = bass.Bass("TRN2", target_bir_lowering=False)

    all_inputs = []

    def din(name, shape, dt=F32):
        ap = nc.dram_tensor(name, shape, dt, kind="ExternalInput").ap()
        all_inputs.append(ap)
        return ap

    x_in = din("x", [D, TC])
    out = nc.dram_tensor("out", [D, TC], F32, kind="ExternalOutput").ap()

    bwT = din("bwT", [D, NST])
    cwT = din("cwT", [NST, D])
    w1T = din("w1T", [D, DFF])
    w3T = din("w3T", [D, DFF])
    w2T = din("w2T", [DFF, D])
    ll0 = din("ll0", [128, 1])
    ll1 = din("ll1", [128, 1])
    ssmw_cols = din("ssmw_cols", [128, NDBLK])
    ffnw_cols = din("ffnw_cols", [128, NDBLK])
    dskip_cols = din("dskip_cols", [128, NDBLK])
    gwT_cols = din("gwT_cols", [128, NDBLK * E])
    wgT = din("wgT", [E * D, EI])
    wuT = din("wuT", [E * D, EI])
    wdT = din("wdT", [E * EI, D])
    wsgT = din("wsgT", [D, SHI])
    wsuT = din("wsuT", [D, SHI])
    wsdT = din("wsdT", [SHI, D])
    winT = din("winT", [D, 2 * DIN])
    convw_cols = din("convw_cols", [128, NDIN * DCONV])
    convb_cols = din("convb_cols", [128, NDIN])
    wxT = din("wxT", [DIN, DTR + 2 * DST])
    wdtT = din("wdtT", [DTR, DIN])
    bdt_cols = din("bdt_cols", [128, NDIN])
    alog_cols = din("alog_cols", [128, NDIN * DST])
    dvec_cols = din("dvec_cols", [128, NDIN])
    woutT = din("woutT", [DIN, D])
    ident32 = din("ident32", [128, 128])
    ones_col = din("ones_col", [128, 1])
    ones_row = din("ones_row", [1, FC])
    ones128 = din("ones128", [128, 128])
    iota8 = din("iota8", [128, E])
    selq = din("selq", [128, NCORES])
    selprev = din("selprev", [128, NCORES])
    mask_last = din("mask_last", [128, TC])

    dscr = nc.dram_tensor
    spill_xi = dscr("spill_xi", [DIN, TC], BF16).ap()
    spill_xc = dscr("spill_xc", [DIN, TC], BF16).ap()
    spill_w = dscr("spill_w", [DIN, TC], BF16).ap()
    spill_z = dscr("spill_z", [DIN, TC], BF16).ap()
    spill_delta = dscr("spill_delta", [DIN, TC], F32).ap()
    spill_bc = dscr("spill_bc", [2 * DST, TC], BF16).ap()
    spill_comb = dscr("spill_comb", [E, TC], F32).ap()
    spill_row = dscr("spill_row", [1, NST], F32).ap()
    cw1 = dscr("cw1", [D, DFF], BF16).ap()
    cw3 = dscr("cw3", [D, DFF], BF16).ap()
    cw2 = dscr("cw2", [DFF, D], BF16).ap()
    cgu = dscr("cgu", [10 * D, EI], BF16).ap()
    cuu = dscr("cuu", [10 * D, EI], BF16).ap()
    cdd = dscr("cdd", [10 * EI, D], BF16).ap()
    cwin = dscr("cwin", [D, 2 * DIN], BF16).ap()
    cwout = dscr("cwout", [DIN, D], BF16).ap()
    cc1_in = dscr("cc1_in", [4, 128], F32).ap()
    cc1_out = dscr("cc1_out", [4 * NCORES, 128], F32).ap()
    cc2_in = dscr("cc2_in", [128, 48], BF16).ap()
    cc2_out = dscr("cc2_out", [128 * NCORES, 48], BF16).ap()
    cc3_in = dscr("cc3_in", [128, 512], F32).ap()
    cc3_out = dscr("cc3_out", [128 * NCORES, 512], F32).ap()

    sy, g, sc, ve, te = nc.sync, nc.gpsimd, nc.scalar, nc.vector, nc.tensor

    ctx = ExitStack()
    with ctx:
        def sbt(name, shape, dt=F32, stack=None):
            return (stack or ctx).enter_context(nc.sbuf_tensor(name, shape, dt))

        X = [sbt(f"X{i}", [128, TC]) for i in range(NDBLK)]
        c_ocol = sbt("c_ocol", [128, 1])
        c_o128 = sbt("c_o128", [128, 128])
        c_o128b = sbt("c_o128b", [1, 128], BF16)
        c_id32 = sbt("c_id32", [128, 128])
        c_idbf = sbt("c_idbf", [128, 128], BF16)
        c_iota = sbt("c_iota", [128, E])
        c_selq = sbt("c_selq", [128, NCORES])
        c_selp = sbt("c_selp", [128, NCORES])
        c_mlast = sbt("c_mlast", [128, TC])
        rows_t = [sbt(f"rows{i}", [1, TC]) for i in range(3)]
        rows = [r[0:1, :] for r in rows_t]
        rows.append(rows[0])
        scr = [sbt("scr0", [128, TC])]
        stg_f = [sbt(f"stg_f{i}", [128, 1024], F32) for i in range(2)]
        stg_b = [sbt(f"stg_b{i}", [128, 1024], BF16) for i in range(2)]
        jfence = sbt("jfence", [128, 4])

        s_sc = Sem(ctx, nc, "s_sc")
        s_ve = Sem(ctx, nc, "s_ve")
        s_te = Sem(ctx, nc, "s_te")
        s_g = Sem(ctx, nc, "s_g")
        s_cc = Sem(ctx, nc, "s_cc")
        ALLS = (s_sc, s_ve, s_te, s_g, s_cc)

        blk = ctx.enter_context(nc.Block())

        class DmaPool:

            def __init__(self, name, eng, n):
                self.sems = [Sem(ctx, nc, f"{name}{i}") for i in range(n)]
                self.eng = eng
                self.i = 0

            def dma(self, dst_ap, src_ap):
                s = self.sems[self.i]
                self.i = (self.i + 1) % len(self.sems)
                if s.v > 0:
                    self.eng.wait_ge(s.h, s.v)
                s.inc(self.eng.dma_start(out=dst_ap, in_=src_ap), 16)
                return (s, s.v)

            def wait(self, eng, h=None):
                if h is not None:
                    eng.wait_ge(h[0].h, h[1])
                else:
                    for s in self.sems:
                        if s.v:
                            eng.wait_ge(s.h, s.v)

        LD = DmaPool("ld", sy, 8)
        ST = DmaPool("st", g, 4)

        def load(dst_ap, src_ap):
            return LD.dma(dst_ap, src_ap)

        def store(dst_ap, src_ap):
            return ST.dma(dst_ap, src_ap)

        def fence_ve():
            s_ve.inc(ve.memset(jfence[:, 0:1], 0.0))

        def fence_sc():
            s_sc.inc(sc.copy(out=jfence[:, 2:3], in_=c_ocol[:, 0:1]))

        def gw_all(eng):
            LD.wait(eng)
            ST.wait(eng)
            for s in ALLS:
                s.wait(eng)

        def barrier():
            fence_ve()
            fence_sc()
            for eng in (sy, sc, ve, te, g):
                gw_all(eng)

        dumped = [False]

        def maybe_dump(tag, tiles):
            if _DEBUG != tag or dumped[0]:
                return
            dumped[0] = True
            barrier()
            for i, t in enumerate(tiles[:8]):
                if t.shape[1] == 1:
                    s_ve.inc(ve.tensor_copy(out=scr[0][:, 0:1], in_=t[:]))
                    s_ve.wait(g)
                    h_ = store(out[i * 128:(i + 1) * 128, 0:2], scr[0][:, 0:2])
                    ST.wait(ve, h_)
                elif t.dtype == BF16:
                    s_ve.inc(ve.tensor_copy(out=scr[0][:, 0:t.shape[1]], in_=t[:]))
                    s_ve.wait(g)
                    h_ = store(out[i * 128:(i + 1) * 128, 0:t.shape[1]], scr[0][:, 0:t.shape[1]])
                    ST.wait(ve, h_)
                else:
                    store(out[i * 128:i * 128 + t.shape[0], 0:t.shape[1]], t[:])
            ST.wait(g)

        bulk_state = {"idx": 0, "marks": {}}

        def bulk_cast(dst, src, rows_n, cols_n):
            CW = 1024
            jobs = []
            for r0 in range(0, rows_n, 128):
                rh = min(128, rows_n - r0)
                for c0 in range(0, cols_n, CW):
                    cwd = min(CW, cols_n - c0)
                    jobs.append((r0, rh, c0, cwd))
            for (r0, rh, c0, cwd) in jobs:
                bi = bulk_state["idx"] % 2
                bulk_state["idx"] += 1
                prev = bulk_state["marks"].get(bi)
                if prev is not None:
                    s_sc.wait(sy, prev[0])
                    ST.wait(sy, prev[1])
                hl = load(stg_f[bi][0:rh, 0:cwd], src[r0:r0 + rh, c0:c0 + cwd])
                LD.wait(sc, hl)
                cast_i = sc.copy(out=stg_b[bi][0:rh, 0:cwd], in_=stg_f[bi][0:rh, 0:cwd])
                s_sc.inc(cast_i)
                s_sc.wait(g)
                hs_ = store(dst[r0:r0 + rh, c0:c0 + cwd], stg_b[bi][0:rh, 0:cwd])
                bulk_state["marks"][bi] = (s_sc.v, hs_)

        def partition_sums(ps_acc, tiles_fn, n_tiles, square):
            gw_all(te)
            if square:
                gw_all(sc)
            for i in range(n_tiles):
                src = tiles_fn(i)
                if square:
                    t = scr[0]
                    s_te.wait(sc)
                    s_sc.inc(sc.activation(out=t[:], in_=src[:], func=AF.Square))
                    s_sc.wait(te)
                    src = t
                for f in range(NT):
                    inst = te.matmul(ps_acc[:, f * FC:(f + 1) * FC], c_ocol[:],
                                     src[:, f * FC:(f + 1) * FC],
                                     start=(i == 0), stop=(i == n_tiles - 1))
                s_te.inc(inst)

        def bcast_row(ps, row_tile, width=TC, off=0):
            bp = row_tile.base_partition()
            for f0 in range(0, width, FC):
                w = min(FC, width - f0)
                inst = te.matmul(ps[:, f0:f0 + w], c_o128[bp:bp + 1, :],
                                 row_tile[0:1, off + f0:off + f0 + w],
                                 start=True, stop=True)
            s_te.inc(inst)

        def ln_rows(ps_sum, ps_sq):
            r_mean, r_tmp, r_rstd = rows[0], rows[1], rows[2]
            s_te.wait(ve)
            s_ve.inc(ve.tensor_scalar(r_mean[:], ps_sum[0:1, :], 1.0 / D, None, ALU.mult))
            s_ve.inc(ve.tensor_scalar(r_tmp[:], ps_sq[0:1, :], 1.0 / D, None, ALU.mult))
            ve.tensor_tensor(r_rstd[:], r_mean[:], r_mean[:], ALU.mult)
            ve.tensor_tensor(r_rstd[:], r_rstd[:], r_tmp[:], ALU.subtract)
            s_ve.inc(ve.tensor_scalar(r_rstd[:], r_rstd[:], -1.0, LN_EPS, ALU.mult, ALU.add))
            s_ve.wait(sc)
            sc.activation(out=r_rstd[:], in_=r_rstd[:], func=AF.Ln)
            s_sc.inc(sc.activation(out=r_rstd[:], in_=r_rstd[:], func=AF.Exp, scale=-0.5))
            return r_rstd, r_mean

        def emit_attn_a():
            bulk_cast(cw1, w1T, D, DFF)
            bulk_cast(cw3, w3T, D, DFF)
            bulk_cast(cw2, w2T, DFF, D)
            pxo = ExitStack()
            px = ExitStack()
            pxo.__enter__()
            with px:
                def psb(name, shape, stack=None):
                    return (stack or px).enter_context(nc.psum_tensor(name, shape, F32))

                with ExitStack() as p1:
                    ps_sum = psb("a_sum", [1, TC], p1)
                    ps_sq = psb("a_sq", [1, TC], p1)
                    partition_sums(ps_sum, lambda i: X[i], NDBLK, square=False)
                    partition_sums(ps_sq, lambda i: X[i], NDBLK, square=True)
                    r_rstd, r_mean = ln_rows(ps_sum, ps_sq)
                    r_rrms, r_mb = rows[1], rows[3]
                    s_sc.wait(ve)
                    ve.tensor_tensor(r_rrms[:], r_rstd[:], r_rstd[:], ALU.mult)
                    s_ve.inc(ve.tensor_scalar(r_rrms[:], r_rrms[:], -LN_EPS, 1.0 + RMS_EPS,
                                              ALU.mult, ALU.add))
                    s_ve.wait(sc)
                    sc.activation(out=r_rrms[:], in_=r_rrms[:], func=AF.Ln)
                    s_sc.inc(sc.activation(out=r_rrms[:], in_=r_rrms[:], func=AF.Exp,
                                           scale=-0.5))
                    s_sc.wait(ve)
                    s_ve.inc(ve.tensor_tensor(r_mb[:], r_mean[:], r_rstd[:], ALU.mult))
                    s_ve.wait(te)

                xln_bf = [sbt(f"xln_bf{i}", [128, TC], BF16, stack=pxo) for i in range(NDBLK)]
                ssw = sbt("ssw", [128, NDBLK], stack=px)
                load(ssw[:], ssmw_cols[:, :])
                dsk = sbt("dsk", [128, NDBLK], stack=px)
                load(dsk[:], dskip_cols[:, :])
                dw = sbt("dw_col", [128, NDBLK], stack=px)
                with ExitStack() as p2:
                    ps_rr = psb("a_rr", [128, TC], p2)
                    ps_mb = psb("a_mb", [128, TC], p2)
                    bcast_row(ps_rr, r_rstd)
                    bcast_row(ps_mb, r_mb)
                    s_te.wait(ve)
                    LD.wait(ve)
                    s_ve.inc(ve.tensor_tensor(dw[:, 0:NDBLK], dsk[:, 0:NDBLK],
                                              ssw[:, 0:NDBLK], ALU.mult))
                    for i in range(NDBLK):
                        ve.tensor_tensor(scr[0][:], X[i][:], ps_rr[:], ALU.mult)
                        s_ve.inc(ve.tensor_tensor(xln_bf[i][:], scr[0][:], ps_mb[:],
                                                  ALU.subtract))
                    s_ve.wait(te)
                    maybe_dump("u", xln_bf)

                bw_f = sbt("bw_f", [128, NST], stack=px)
                bw_b = [sbt(f"bw_b{i}", [128, NST], BF16, stack=px) for i in range(NDBLK)]
                for i in range(NDBLK):
                    s_ve.wait(sy)
                    hl = load(bw_f[:], bwT[i * 128:(i + 1) * 128, :])
                    LD.wait(ve, hl)
                    s_ve.inc(ve.tensor_scalar(bw_b[i][:], bw_f[:], ssw[:, i:i + 1],
                                              None, ALU.mult))
                Bu = [sbt(f"Bu{i}", [128, TC], stack=px) for i in range(2)]
                rm_sb = sbt("a_rmsb", [128, TC], BF16, stack=px)
                p_bu = ExitStack()
                ps_bu = psb("a_bu", [128, FC], p_bu)
                gw_all(te)
                with ExitStack() as p_rm:
                    ps_rm = psb("a_rm", [128, TC], p_rm)
                    bcast_row(ps_rm, rows[1])
                    s_te.wait(ve)
                    s_ve.inc(ve.tensor_copy(out=rm_sb[:], in_=ps_rm[:]))
                    s_ve.wait(te)
                for p in range(2):
                    for f in range(NT):
                        for k in range(NDBLK):
                            inst = te.matmul(ps_bu[:], bw_b[k][:, p * 128:(p + 1) * 128],
                                             xln_bf[k][:, f * FC:(f + 1) * FC],
                                             start=(k == 0), stop=(k == NDBLK - 1))
                        s_te.inc(inst)
                        s_te.wait(sc)
                        s_sc.inc(sc.copy(out=Bu[p][:, f * FC:(f + 1) * FC], in_=ps_bu[:]))
                        s_sc.wait(te)
                s_sc.wait(ve)
                s_te.wait(ve)
                for p in range(2):
                    s_ve.inc(ve.tensor_tensor(Bu[p][:], Bu[p][:], rm_sb[:], ALU.mult))
                s_ve.wait(te)
                p_bu.close()

                maybe_dump("bu", Bu)
                lam_p = [sbt(f"lam_p{j}", [128, 1], stack=px) for j in range(2)]
                sp_p = [sbt(f"sp_p{j}", [128, 1], stack=px) for j in range(2)]
                tr1 = sbt("tr1", [128, 1], stack=px)
                for j in range(2):
                    hl = load(lam_p[j][:], (ll0 if j == 0 else ll1)[:, :])
                    LD.wait(sc, hl)
                    sc.activation(out=sp_p[j][:], in_=lam_p[j][:], func=AF.Exp,
                                  scale=-1.0)
                    sc.activation(out=sp_p[j][:], in_=sp_p[j][:], func=AF.Ln, bias=1.0)
                    sc.activation(out=sp_p[j][:], in_=sp_p[j][:], func=AF.Exp,
                                  scale=-float(TC))
                    s_sc.inc(sc.activation(out=lam_p[j][:], in_=lam_p[j][:],
                                           func=AF.Sigmoid))
                fence_sc()

                maybe_dump("lam", [lam_p[0][:], lam_p[1][:], sp_p[0][:], sp_p[1][:],
                                   Bu[0][:], Bu[1][:]])
                sp_cols = sbt("sp_cols", [128, 4], stack=px)
                hs = sbt("hs_a", [128, TC], stack=px)
                s_sc.wait(ve)
                for p in range(2):
                    ve.tensor_tensor_scan(
                        hs[:], lam_p[p][:, 0:1].broadcast_to((128, TC)), Bu[p][:],
                        0.0, ALU.mult, ALU.add)
                    s_ve.inc(ve.scalar_tensor_tensor(scr[0][:], hs[:], 1.0, c_mlast[:],
                                                     ALU.bypass, ALU.mult,
                                                     accum_out=sp_cols[:, p:p + 1]))
                maybe_dump("ph1", [hs[:]])
                s_sc.wait(ve)
                for j in range(2):
                    s_ve.inc(ve.tensor_copy(out=sp_cols[:, 2 + j:3 + j], in_=sp_p[j][:]))
                fence_ve()
                p_t = ExitStack()
                ps_t1 = psb("a_t1", [4, 128], p_t)
                sp_rows = sbt("sp_rows", [4, 128], stack=px)
                s_ve.wait(te)
                s_te.inc(te.transpose(ps_t1[:], sp_cols[:], c_id32[:]))
                s_te.wait(sc)
                s_sc.inc(sc.copy(out=sp_rows[:], in_=ps_t1[:]))
                fence_sc()
                s_sc.wait(g)
                store(cc1_in[:, :], sp_rows[:])
                ST.wait(g)
                s_cc.inc(g.collective_compute(
                    "AllGather", ALU.bypass, replica_groups=[list(range(NCORES))],
                    ins=[cc1_in.opt()], outs=[cc1_out.opt()]))
                s_cc.wait(g)
                for _d in range(20):
                    s_g.inc(g.memset(scr[0][:, 0:2048], 0.0))
                growA = sbt("a_growA", [16, 128], stack=px)
                growB = sbt("a_growB", [16, 128], stack=px)
                hgrA = store(growA[:], cc1_out[0:16, :])
                hgrB = store(growB[:], cc1_out[16:32, :])
                ps_t2 = psb("a_t2", [128, 32], p_t)
                gA = sbt("a_gA", [128, 16], stack=px)
                gB = sbt("a_gB", [128, 16], stack=px)
                ST.wait(te, hgrA)
                ST.wait(te, hgrB)
                s_te.inc(te.transpose(ps_t2[:, 0:16], growA[:], c_id32[0:16, 0:16]))
                s_te.inc(te.transpose(ps_t2[:, 16:32], growB[:], c_id32[0:16, 0:16]))
                s_te.wait(ve)
                s_ve.inc(ve.tensor_copy(out=gA[:], in_=ps_t2[:, 0:16]))
                s_ve.inc(ve.tensor_copy(out=gB[:], in_=ps_t2[:, 16:32]))
                fence_ve()
                s_ve.wait(te)
                p_t.close()
                Hc = [sbt(f"aH{j}", [128, 2], stack=px) for j in range(NCORES)]
                hstart_c = sbt("a_hst_c", [128, 2], stack=px)
                for b_ in range(2):
                    gsrc = gA if b_ == 0 else gB
                    s_ve.inc(ve.memset(Hc[b_ * 4][:], 0.0))
                    fence_ve()
                    for k in range(1, 4):
                        j, r = b_ * 4 + k, k - 1
                        ve.tensor_tensor(Hc[j][:], Hc[j - 1][:],
                                         gsrc[:, 4 * r + 2:4 * r + 4], ALU.mult)
                        fence_ve()
                        s_ve.inc(ve.tensor_tensor(Hc[j][:], Hc[j][:],
                                                  gsrc[:, 4 * r:4 * r + 2], ALU.add))
                        fence_ve()
                s_ve.inc(ve.tensor_scalar(hstart_c[:], Hc[0][:], c_selq[:, 0:1],
                                          None, ALU.mult))
                fence_ve()
                for j in range(1, NCORES):
                    s_ve.inc(ve.scalar_tensor_tensor(hstart_c[:], Hc[j][:],
                                                     c_selq[:, j:j + 1], hstart_c[:],
                                                     ALU.mult, ALU.add))
                    fence_ve()
                hstart_c2 = sbt("a_hst_c2", [128, 2], stack=px)
                s_ve.wait(sc)
                s_sc.inc(sc.copy(out=hstart_c2[:], in_=hstart_c[:]))
                fence_sc()
                hstart_c = hstart_c2
                maybe_dump("xb", X)
                maybe_dump("carry", [hstart_c[:], Hc[1][:], Hc[2][:], Hc[3][:],
                                     Hc[5][:], Hc[6][:], Hc[7][:], gA[:]])
                maybe_dump("carry2", [sp_cols[:], sp_rows[:], gA[:], gB[:]])
                hs_bf = [sbt(f"hs_bf{i}", [128, TC], BF16, stack=px) for i in range(2)]
                s_g.wait(ve)
                s_sc.wait(ve)
                for p in range(2):
                    ve.tensor_tensor_scan(hs[:],
                                          lam_p[p][:, 0:1].broadcast_to((128, TC)),
                                          Bu[p][:], hstart_c[:, p:p + 1],
                                          ALU.mult, ALU.add)
                    s_ve.inc(ve.tensor_copy(out=hs_bf[p][:], in_=hs[:]))

                maybe_dump("hs", hs_bf)
                cw_f = stg_f[0]
                cw_b = [sbt(f"cw_b{i}", [128, D], BF16, stack=px) for i in range(2)]
                for p in range(2):
                    s_sc.wait(sy)
                    hl = load(cw_f[:, 0:D], cwT[p * 128:(p + 1) * 128, :])
                    LD.wait(sc, hl)
                    s_sc.inc(sc.copy(out=cw_b[p][:], in_=cw_f[:, 0:D]))
                s_bf = xln_bf
                ps_y = psb("a_y", [128, FC])
                ps_s2 = psb("a_s2", [1, TC])
                gw_all(te)
                gw_all(sc)
                for i in range(NDBLK):
                    for f in range(NT):
                        fsl = slice(f * FC, (f + 1) * FC)
                        for k in range(2):
                            inst = te.matmul(ps_y[:], cw_b[k][:, i * 128:(i + 1) * 128],
                                             hs_bf[k][:, fsl],
                                             start=(k == 0), stop=(k == 1))
                        s_te.inc(inst)
                        s_te.wait(ve)
                        s_sc.wait(ve)
                        ve.tensor_tensor(scr[0][:, 0:FC], xln_bf[i][:, fsl],
                                         rm_sb[:, fsl], ALU.mult)
                        ve.scalar_tensor_tensor(scr[0][:, FC:2 * FC], scr[0][:, 0:FC],
                                                dw[:, i:i + 1], ps_y[:],
                                                ALU.mult, ALU.add)
                        ve.tensor_tensor(scr[0][:, 0:FC], scr[0][:, FC:2 * FC],
                                         xln_bf[i][:, fsl], ALU.add)
                        ve.tensor_tensor(X[i][:, fsl], X[i][:, fsl],
                                         scr[0][:, 0:FC], ALU.add)
                        s_ve.inc(ve.tensor_copy(out=s_bf[i][:, fsl],
                                                in_=scr[0][:, 0:FC]))
                        s_ve.wait(sc)
                        s_sc.inc(sc.activation(out=scr[0][:, FC:2 * FC],
                                               in_=scr[0][:, 0:FC], func=AF.Square))
                        s_sc.wait(te)
                        inst = te.matmul(ps_s2[:, fsl], c_ocol[:],
                                         scr[0][:, FC:2 * FC],
                                         start=(i == 0), stop=(i == NDBLK - 1))
                        s_te.inc(inst)
                        s_te.wait(ve)
                        s_te.wait(sc)

            maybe_dump("combo", [hs_bf[0][:], hs_bf[1][:], X[0][:], X[1][:],
                                 X[2][:], X[3][:]])
            px = ExitStack()
            with px:
                def psb(name, shape, stack=None):
                    return (stack or px).enter_context(nc.psum_tensor(name, shape, F32))

                with ExitStack() as p1:
                    s_te.wait(ve)
                    s_ve.inc(ve.tensor_scalar(rows[0][:], ps_s2[0:1, :], 1.0 / D, RMS_EPS,
                                              ALU.mult, ALU.add))
                    s_ve.wait(sc)
                    sc.activation(out=rows[0][:], in_=rows[0][:], func=AF.Ln)
                    s_sc.inc(sc.activation(out=rows[0][:], in_=rows[0][:], func=AF.Exp,
                                           scale=-0.5))
                    s_sc.wait(te)
                    s_ve.wait(te)
                z2 = s_bf
                with ExitStack() as p2:
                    ps_rb = psb("f_rb", [128, TC], p2)
                    s_sc.wait(te)
                    bcast_row(ps_rb, rows[0])
                    fwc = sbt("fwc", [128, NDBLK], stack=px)
                    load(fwc[:], ffnw_cols[:, :])
                    s_te.wait(ve)
                    LD.wait(ve)
                    for i in range(NDBLK):
                        ve.tensor_scalar(scr[0][:], s_bf[i][:], fwc[:, i:i + 1],
                                         None, ALU.mult)
                        s_ve.inc(ve.tensor_tensor(z2[i][:], scr[0][:], ps_rb[:], ALU.mult))
                    s_ve.wait(te)

                maybe_dump("z2", z2)
                mid = sbt("ffn_mid", [128, NP_FF * FC], BF16, stack=px)
                w1s = [sbt(f"w1s{i}", [128, 1024], BF16, stack=px) for i in range(2)]
                w3s = [sbt(f"w3s{i}", [128, 1024], BF16, stack=px) for i in range(2)]
                w2s = [sbt(f"w2s{i}", [128, 128], BF16, stack=px) for i in range(2)]
                ps_g1 = psb("f_g1", [128, FC])
                ps_v1 = psb("f_v1", [128, FC])
                ps_oA = psb("f_oA", [128, FC])
                gsil = sbt("f_gsil", [128, FC], BF16, stack=px)
                gw_all(sy)

                def pw_of(p):
                    return 128 if p < NP_FF - 1 else DFF - 128 * (NP_FF - 1)

                def loadA(p):
                    bi = p % 2
                    pw = pw_of(p)
                    hs_ = []
                    for k in range(NDBLK):
                        hs_.append(load(w1s[bi][:, k * 128:k * 128 + pw],
                                        cw1[k * 128:(k + 1) * 128, p * 128:p * 128 + pw]))
                        hs_.append(load(w3s[bi][:, k * 128:k * 128 + pw],
                                        cw3[k * 128:(k + 1) * 128, p * 128:p * 128 + pw]))
                    return hs_

                def loadB(po, k):
                    return load(w2s[k % 2][0:kw_of(k), :],
                                cw2[k * 128:k * 128 + kw_of(k), po * 128:(po + 1) * 128])

                def kw_of(k):
                    return 128 if k < NP_FF - 1 else DFF - 128 * (NP_FF - 1)

                for tq in range(NT):
                    fc0 = tq * FC
                    te_mark = {}
                    s_te.wait(sy)
                    ld_mark = {0: loadA(0)}
                    for p in range(NP_FF):
                        pw = pw_of(p)
                        bi = p % 2
                        if p + 1 < NP_FF:
                            if p >= 1:
                                s_te.wait(sy, te_mark[p - 1])
                            ld_mark[p + 1] = loadA(p + 1)
                        for h_ in ld_mark[p]:
                            LD.wait(te, h_)
                        for k in range(NDBLK):
                            te.matmul(ps_g1[0:pw, :], w1s[bi][:, k * 128:k * 128 + pw],
                                      z2[k][:, fc0:fc0 + FC],
                                      start=(k == 0), stop=(k == NDBLK - 1))
                        for k in range(NDBLK):
                            inst = te.matmul(ps_v1[0:pw, :], w3s[bi][:, k * 128:k * 128 + pw],
                                             z2[k][:, fc0:fc0 + FC],
                                             start=(k == 0), stop=(k == NDBLK - 1))
                        s_te.inc(inst)
                        te_mark[p] = s_te.v
                        s_te.wait(sc)
                        s_sc.inc(sc.activation(out=gsil[0:pw, :], in_=ps_g1[0:pw, :],
                                               func=AF.Silu))
                        s_sc.wait(ve)
                        s_ve.inc(ve.tensor_tensor(mid[0:pw, p * FC:(p + 1) * FC],
                                                  gsil[0:pw, :], ps_v1[0:pw, :], ALU.mult))
                        s_ve.wait(te)
                        s_ve.wait(sc)
                    for po in range(NDBLK):
                        teB = {}
                        s_te.wait(sy)
                        ldB = {0: loadB(po, 0)}
                        for k in range(NP_FF):
                            if k + 1 < NP_FF:
                                if k >= 1:
                                    s_te.wait(sy, teB[k - 1])
                                ldB[k + 1] = loadB(po, k + 1)
                            LD.wait(te, ldB[k])
                            inst = te.matmul(ps_oA[:], w2s[k % 2][0:kw_of(k), :],
                                             mid[0:kw_of(k), k * FC:(k + 1) * FC],
                                             start=(k == 0), stop=(k == NP_FF - 1))
                            s_te.inc(inst)
                            teB[k] = s_te.v
                        s_te.wait(ve)
                        s_ve.inc(ve.tensor_tensor(X[po][:, fc0:fc0 + FC],
                                                  X[po][:, fc0:fc0 + FC],
                                                  ps_oA[:], ALU.add))
                        s_ve.wait(te)
            pxo.__exit__(None, None, None)


        def emit_moe(first):
            if first:
                for e in range(E):
                    bulk_cast(cgu[e * D:(e + 1) * D, :], wgT[e * D:(e + 1) * D, :], D, EI)
                    bulk_cast(cuu[e * D:(e + 1) * D, :], wuT[e * D:(e + 1) * D, :], D, EI)
                    bulk_cast(cdd[e * EI:(e + 1) * EI, :], wdT[e * EI:(e + 1) * EI, :], EI, D)
                bulk_cast(cgu[8 * D:9 * D, :], wsgT[:, 0:EI], D, EI)
                bulk_cast(cgu[9 * D:10 * D, :], wsgT[:, EI:SHI], D, EI)
                bulk_cast(cuu[8 * D:9 * D, :], wsuT[:, 0:EI], D, EI)
                bulk_cast(cuu[9 * D:10 * D, :], wsuT[:, EI:SHI], D, EI)
                bulk_cast(cdd[8 * EI:9 * EI, :], wsdT[0:EI, :], EI, D)
                bulk_cast(cdd[9 * EI:10 * EI, :], wsdT[EI:SHI, :], EI, D)
            px = ExitStack()
            with px:
                def psb(name, shape, stack=None):
                    return (stack or px).enter_context(nc.psum_tensor(name, shape, F32))

                tag = "m1" if first else "m2"
                z_bf = [sbt(f"{tag}_z{i}", [128, TC], BF16, stack=px) for i in range(NDBLK)]
                comb_sb = sbt(f"{tag}_comb", [128, 128], stack=px)
                with ExitStack() as p1:
                    ps_sum = psb(f"{tag}_sum", [1, TC], p1)
                    ps_sq = psb(f"{tag}_sq", [1, TC], p1)
                    partition_sums(ps_sum, lambda i: X[i], NDBLK, square=False)
                    partition_sums(ps_sq, lambda i: X[i], NDBLK, square=True)
                    r_rstd, r_mean = ln_rows(ps_sum, ps_sq)
                    s_sc.wait(ve)
                    s_ve.inc(ve.tensor_tensor(rows[1][:], r_mean[:], r_rstd[:], ALU.mult))
                    s_ve.wait(te)
                with ExitStack() as p2:
                    ps_rr = psb(f"{tag}_rr", [128, HF], p2)
                    ps_mb = psb(f"{tag}_mb", [128, HF], p2)
                    ps_lg = psb(f"{tag}_lg", [128, 64], p2)
                    acc_lg = sbt(f"{tag}_acc", [128, 128], stack=px)
                    gw_sb = sbt(f"{tag}_gw", [128, NDBLK * E], stack=px)
                    load(gw_sb[:], gwT_cols[:, :])
                    LD.wait(te)
                    for hh in range(2):
                        h0 = hh * HF
                        s_ve.wait(te)
                        bcast_row(ps_rr, rows[2], width=HF, off=h0)
                        bcast_row(ps_mb, rows[1], width=HF, off=h0)
                        s_te.wait(ve)
                        for i in range(NDBLK):
                            ve.tensor_tensor(scr[0][:, 0:HF], X[i][:, h0:h0 + HF],
                                             ps_rr[:], ALU.mult)
                            ve.tensor_tensor(scr[0][:, 0:HF], scr[0][:, 0:HF],
                                             ps_mb[:], ALU.subtract)
                            s_ve.inc(ve.tensor_copy(out=z_bf[i][:, h0:h0 + HF],
                                                    in_=scr[0][:, 0:HF]))
                            s_ve.wait(te)
                            for tp in range(8):
                                inst = te.matmul(ps_lg[:, tp * 8:(tp + 1) * 8],
                                                 scr[0][:, tp * 128:(tp + 1) * 128],
                                                 gw_sb[:, i * 8:(i + 1) * 8],
                                                 start=True, stop=True)
                            s_te.inc(inst)
                            s_te.wait(ve)
                            asl = acc_lg[:, hh * 64:(hh + 1) * 64]
                            if i == 0:
                                s_ve.inc(ve.tensor_copy(out=asl, in_=ps_lg[:]))
                            else:
                                s_ve.inc(ve.tensor_tensor(asl, asl, ps_lg[:], ALU.add))
                            s_ve.wait(te)
                    sc_t = sbt(f"{tag}_sct", [128, 128], stack=px)
                    s_ve.wait(sc)
                    s_sc.inc(sc.activation(out=sc_t[:], in_=acc_lg[:], func=AF.Sigmoid))
                    fence_sc()
                    tk = [sbt(f"{tag}_tk{i}", [128, 8], stack=px) for i in range(4)]
                    tcl = [sbt(f"{tag}_tc{i}", [128, 1], stack=px) for i in range(4)]
                    s_sc.wait(ve)

                    def vb(inst, ap_):
                        s_ve.inc(inst)
                        s_ve.wait(sc)
                        s_sc.inc(sc.copy(out=ap_, in_=ap_))
                        fence_sc()
                        s_sc.wait(ve)

                    for tp in range(16):
                        sl = sc_t[:, tp * 8:(tp + 1) * 8]
                        vb(ve.tensor_reduce(tcl[0][:], sl, AX, ALU.max), tcl[0][:])
                        vb(ve.tensor_scalar(tk[0][:], sl, tcl[0][:, 0:1], None,
                                            ALU.is_equal), tk[0][:])
                        vb(ve.scalar_tensor_tensor(tk[1][:], tk[0][:], BIG,
                                                   c_iota[:, 0:8],
                                                   ALU.mult, ALU.subtract), tk[1][:])
                        vb(ve.tensor_reduce(tcl[1][:], tk[1][:], AX, ALU.max),
                           tcl[1][:])
                        vb(ve.tensor_scalar(tk[0][:], tk[1][:], tcl[1][:, 0:1], None,
                                            ALU.is_equal), tk[0][:])
                        vb(ve.scalar_tensor_tensor(tk[2][:], tk[0][:], -BIG, sl,
                                                   ALU.mult, ALU.add), tk[2][:])
                        vb(ve.tensor_reduce(tcl[2][:], tk[2][:], AX, ALU.max),
                           tcl[2][:])
                        vb(ve.tensor_scalar(tk[3][:], tk[2][:], tcl[2][:, 0:1], None,
                                            ALU.is_equal), tk[3][:])
                        vb(ve.scalar_tensor_tensor(tk[3][:], tk[3][:], BIG,
                                                   c_iota[:, 0:8],
                                                   ALU.mult, ALU.subtract), tk[3][:])
                        vb(ve.tensor_reduce(tcl[3][:], tk[3][:], AX, ALU.max), tcl[3][:])
                        vb(ve.tensor_scalar(tk[3][:], tk[3][:], tcl[3][:, 0:1], None,
                                            ALU.is_equal), tk[3][:])
                        vb(ve.tensor_tensor(tcl[1][:], tcl[0][:], tcl[2][:], ALU.add),
                           tcl[1][:])
                        vb(ve.tensor_scalar(tcl[1][:], tcl[1][:], 1e-20, None, ALU.add),
                           tcl[1][:])
                        vb(ve.reciprocal(tcl[1][:], tcl[1][:]), tcl[1][:])
                        vb(ve.tensor_scalar(tcl[1][:], tcl[1][:], MOE_SCALE, None,
                                            ALU.mult), tcl[1][:])
                        vb(ve.tensor_tensor(tcl[0][:], tcl[0][:], tcl[1][:], ALU.mult),
                           tcl[0][:])
                        vb(ve.tensor_tensor(tcl[2][:], tcl[2][:], tcl[1][:], ALU.mult),
                           tcl[2][:])
                        vb(ve.tensor_scalar(tk[3][:], tk[3][:], tcl[2][:, 0:1], None,
                                            ALU.mult), tk[3][:])
                        s_ve.inc(ve.scalar_tensor_tensor(
                            comb_sb[:, tp * 8:(tp + 1) * 8], tk[0][:], tcl[0][:, 0:1],
                            tk[3][:], ALU.mult, ALU.add))
                        fence_ve()
                    s_ve.wait(te)
                with ExitStack() as p3:
                    ps_cmb = psb(f"{tag}_cmb", [8, TC], p3)
                    cmb_rows = sbt(f"{tag}_cmbr", [8, TC], stack=p3)
                    s_ve.wait(te)
                    for tp in range(16):
                        inst = te.transpose(ps_cmb[:, tp * 128:(tp + 1) * 128],
                                            comb_sb[:, tp * 8:(tp + 1) * 8], c_id32[:])
                    s_te.inc(inst)
                    s_te.wait(sc)
                    s_sc.inc(sc.copy(out=cmb_rows[:], in_=ps_cmb[:]))
                    fence_sc()
                    s_sc.wait(g)
                    hst_ = store(spill_comb[:, :], cmb_rows[:])
                    s_sc.wait(te)

                wg_s = [sbt(f"{tag}_wg0", [128, NDBLK * EI], BF16, stack=px)] * 2
                wu_s = [sbt(f"{tag}_wu0", [128, NDBLK * EI], BF16, stack=px)] * 2
                wd_s = sbt(f"{tag}_wd", [128, 4 * D], BF16, stack=px)
                mid_bf = [sbt(f"{tag}_mid{p}", [128, HF], BF16, stack=px)
                          for p in range(4)]
                gs_bf = sbt(f"{tag}_gs", [128, FC], BF16, stack=px)
                cb_f = sbt(f"{tag}_cb", [128, FC], stack=px)
                ps_g1 = psb(f"{tag}_g1", [128, FC])
                ps_u1 = psb(f"{tag}_u1", [128, FC])
                ps_o1 = psb(f"{tag}_o1", [128, FC])

                def load_slot(s):
                    bi = s % 2
                    hh = []
                    for k in range(NDBLK):
                        hh.append(load(wg_s[bi][:, k * EI:(k + 1) * EI],
                                       cgu[s * D + k * 128:s * D + (k + 1) * 128, :]))
                        hh.append(load(wu_s[bi][:, k * EI:(k + 1) * EI],
                                       cuu[s * D + k * 128:s * D + (k + 1) * 128, :]))
                    return hh

                def load_wd(s):
                    hh = []
                    for k in range(4):
                        hh.append(load(wd_s[:, k * D:(k + 1) * D],
                                       cdd[s * EI + k * 128:s * EI + (k + 1) * 128, :]))
                    return hh

                gw_all(sy)
                ST.wait(sy, hst_)
                slot_mark = {}
                for s in range(10):
                    bi = 0
                    if s >= 1:
                        s_te.wait(sy)
                    lm = load_slot(s)
                    for h_ in lm:
                        LD.wait(te, h_)
                    first_wd = True
                    for hh in range(2):
                        h0 = hh * HF
                        for p in range(4):
                            for tcl_ in range(2):
                                fsl = slice(h0 + tcl_ * FC, h0 + (tcl_ + 1) * FC)
                                msl = slice(tcl_ * FC, (tcl_ + 1) * FC)
                                for k in range(NDBLK):
                                    te.matmul(ps_g1[:], wg_s[bi][:, k * EI + p * 128:
                                                                 k * EI + (p + 1) * 128],
                                              z_bf[k][:, fsl],
                                              start=(k == 0), stop=(k == NDBLK - 1))
                                for k in range(NDBLK):
                                    inst = te.matmul(ps_u1[:],
                                                     wu_s[bi][:, k * EI + p * 128:
                                                              k * EI + (p + 1) * 128],
                                                     z_bf[k][:, fsl],
                                                     start=(k == 0), stop=(k == NDBLK - 1))
                                s_te.inc(inst)
                                s_te.wait(sc)
                                s_sc.inc(sc.activation(out=gs_bf[:], in_=ps_g1[:],
                                                       func=AF.Silu))
                                s_sc.wait(ve)
                                s_ve.inc(ve.tensor_tensor(mid_bf[p][:, msl], gs_bf[:],
                                                          ps_u1[:], ALU.mult))
                                s_ve.wait(te)
                                s_ve.wait(sc)
                        if s < E:
                            for tcl_ in range(2):
                                fsl = slice(h0 + tcl_ * FC, h0 + (tcl_ + 1) * FC)
                                msl = slice(tcl_ * FC, (tcl_ + 1) * FC)
                                s_ve.wait(g)
                                hcb = store(cb_f[:],
                                            spill_comb[s:s + 1, fsl]
                                            .partition_broadcast(128))
                                ST.wait(ve, hcb)
                                for p in range(4):
                                    s_ve.inc(ve.tensor_tensor(mid_bf[p][:, msl],
                                                              mid_bf[p][:, msl],
                                                              cb_f[:], ALU.mult))
                        if first_wd:
                            s_te.wait(sy)
                            hwd = load_wd(s)
                            for h_ in hwd:
                                LD.wait(te, h_)
                            first_wd = False
                        s_ve.wait(te)
                        for po in range(NDBLK):
                            for tcl_ in range(2):
                                fsl = slice(h0 + tcl_ * FC, h0 + (tcl_ + 1) * FC)
                                msl = slice(tcl_ * FC, (tcl_ + 1) * FC)
                                for k in range(4):
                                    inst = te.matmul(ps_o1[:],
                                                     wd_s[:, k * D + po * 128:
                                                          k * D + (po + 1) * 128],
                                                     mid_bf[k][:, msl],
                                                     start=(k == 0), stop=(k == 3))
                                s_te.inc(inst)
                                s_te.wait(ve)
                                s_ve.inc(ve.tensor_tensor(X[po][:, fsl], X[po][:, fsl],
                                                          ps_o1[:], ALU.add))
                                s_ve.wait(te)
                    slot_mark[s] = s_te.v


        def emit_attn_b():
            bulk_cast(cwin, winT, D, 2 * DIN)
            bulk_cast(cwout, woutT, DIN, D)
            px = ExitStack()
            with px:
                def psb(name, shape, stack=None):
                    return (stack or px).enter_context(nc.psum_tensor(name, shape, F32))

                halo_sb = sbt("b_halo", [128, 48], BF16, stack=px)
                pA = ExitStack()
                xb_bf = [sbt(f"b_x{i}", [128, TC], BF16, stack=pA) for i in range(NDBLK)]
                with ExitStack() as p1:
                    ps_sum = psb("b_sum", [1, TC], p1)
                    ps_sq = psb("b_sq", [1, TC], p1)
                    partition_sums(ps_sum, lambda i: X[i], NDBLK, square=False)
                    partition_sums(ps_sq, lambda i: X[i], NDBLK, square=True)
                    r_rstd, r_mean = ln_rows(ps_sum, ps_sq)
                    s_sc.wait(ve)
                    s_ve.inc(ve.tensor_tensor(rows[1][:], r_mean[:], r_rstd[:], ALU.mult))
                    s_ve.wait(te)
                with ExitStack() as p2:
                    ps_rr = psb("b_rr", [128, HF], p2)
                    ps_mb = psb("b_mb", [128, HF], p2)
                    for hh in range(2):
                        h0 = hh * HF
                        s_ve.wait(te)
                        bcast_row(ps_rr, rows[2], width=HF, off=h0)
                        bcast_row(ps_mb, rows[1], width=HF, off=h0)
                        s_te.wait(ve)
                        for i in range(NDBLK):
                            ve.tensor_tensor(scr[0][:, 0:HF], X[i][:, h0:h0 + HF],
                                             ps_rr[:], ALU.mult)
                            ve.tensor_tensor(scr[0][:, 0:HF], scr[0][:, 0:HF],
                                             ps_mb[:], ALU.subtract)
                            s_ve.inc(ve.tensor_copy(out=xb_bf[i][:, h0:h0 + HF],
                                                    in_=scr[0][:, 0:HF]))
                        s_ve.wait(te)

                gw_all(sy)
                wsl = [sbt(f"b_wsl{i}", [128, NDBLK * FC], BF16, stack=pA)
                       for i in range(2)]
                stg = sbt("b_stage", [128, FC], BF16, stack=pA)
                ps_xz = psb("b_xz", [128, FC], pA)

                def load_slab(sl):
                    hh = []
                    for k in range(NDBLK):
                        hh.append(load(wsl[sl % 2][:, k * FC:(k + 1) * FC],
                                       cwin[k * 128:(k + 1) * 128,
                                            sl * FC:(sl + 1) * FC]))
                    return hh

                slab_mark = {}
                stg_h = [None]
                lm = {0: load_slab(0)}
                for sl in range(8):
                    if sl + 1 < 8:
                        if sl >= 1:
                            s_te.wait(sy, slab_mark[sl - 1])
                        lm[sl + 1] = load_slab(sl + 1)
                    for h_ in lm[sl]:
                        LD.wait(te, h_)
                    for p in range(4):
                        orow = sl * FC + p * 128
                        for tc in range(NT):
                            fsl = slice(tc * FC, (tc + 1) * FC)
                            for k in range(NDBLK):
                                inst = te.matmul(
                                    ps_xz[:],
                                    wsl[sl % 2][:, k * FC + p * 128:k * FC + (p + 1) * 128],
                                    xb_bf[k][:, fsl],
                                    start=(k == 0), stop=(k == NDBLK - 1))
                            s_te.inc(inst)
                            s_te.wait(sc)
                            if stg_h[0] is not None:
                                ST.wait(sc, stg_h[0])
                            s_sc.inc(sc.copy(out=stg[:], in_=ps_xz[:]))
                            fence_sc()
                            s_sc.wait(te)
                            s_sc.wait(g)
                            if orow < DIN:
                                stg_h[0] = store(spill_xi[orow:orow + 128, fsl], stg[:])
                                if tc == NT - 1:
                                    dk = orow // 128
                                    s_sc.wait(ve)
                                    s_ve.inc(ve.tensor_copy(
                                        out=halo_sb[:, dk * 3:dk * 3 + 3],
                                        in_=stg[:, FC - 3:FC]))
                            else:
                                stg_h[0] = store(spill_z[orow - DIN:orow - DIN + 128, fsl],
                                                 stg[:])
                            s_ve.wait(sc)
                    slab_mark[sl] = s_te.v

                gw_all(sy)
                gw_all(g)
                pA.close()
                s_ve.wait(sc)
                s_sc.inc(sc.copy(out=halo_sb[:], in_=halo_sb[:]))
                fence_sc()
                s_sc.wait(g)
                s_ve.wait(g)
                hh_ = store(cc2_in[:, :], halo_sb[:])
                ST.wait(g)
                s_cc.inc(g.collective_compute(
                    "AllGather", ALU.bypass, replica_groups=[list(range(NCORES))],
                    ins=[cc2_in.opt()], outs=[cc2_out.opt()]))
                s_cc.wait(g)
                halo_rv = sbt("b_halorv", [128, 48], BF16, stack=px)
                hblk = sbt("b_hblk", [128, 48], BF16, stack=px)
                s_ve.wait(g)
                s_g.inc(g.memset(halo_rv[:], 0.0))
                s_g.wait(ve)
                for r in range(NCORES):
                    hx = store(hblk[:], cc2_out[r * 128:(r + 1) * 128, :])
                    ST.wait(ve, hx)
                    s_ve.inc(ve.scalar_tensor_tensor(halo_rv[:], hblk[:],
                                                     c_selp[:, r:r + 1], halo_rv[:],
                                                     ALU.mult, ALU.add))
                    s_ve.wait(g)

                dsum = sbt("b_dsum", [128, NDIN], stack=px)
                pB = ExitStack()
                wx_bf = sbt("b_wx", [128, NDIN * 96], BF16, stack=pB)
                wdt_bf = sbt("b_wdt", [64, DIN], BF16, stack=pB)
                for k in range(NDIN):
                    hl = load(stg_f[0][:, 0:96], wxT[k * 128:(k + 1) * 128, :])
                    LD.wait(sc, hl)
                    s_sc.inc(sc.copy(out=wx_bf[:, k * 96:(k + 1) * 96],
                                     in_=stg_f[0][:, 0:96]))
                    s_sc.wait(sy)
                for hf_ in range(2):
                    hl = load(stg_f[0][0:64, 0:HF], wdtT[:, hf_ * HF:(hf_ + 1) * HF])
                    LD.wait(sc, hl)
                    s_sc.inc(sc.copy(out=wdt_bf[:, hf_ * HF:(hf_ + 1) * HF],
                                     in_=stg_f[0][0:64, 0:HF]))
                    s_sc.wait(sy)
                bdt_sb = sbt("b_bdt", [128, NDIN], stack=pB)
                load(bdt_sb[:], bdt_cols[:, :])
                cvw = sbt("b_cvw", [128, NDIN * DCONV], stack=pB)
                load(cvw[:], convw_cols[:, :])
                cvb = sbt("b_cvb", [128, NDIN], stack=pB)
                load(cvb[:], convb_cols[:, :])
                Brow = sbt("b_Brow", [16, TC], BF16, stack=pB)
                Crow = sbt("b_Crow", [16, TC], BF16, stack=pB)

                xiw = sbt("b_xiw", [128, FC + 3], BF16, stack=pB)
                w_bf = sbt("b_wbf", [128, FC], BF16, stack=pB)
                xc_bf = [sbt(f"b_xc{k}", [128, FC], BF16, stack=pB) for k in range(NDIN)]
                dtr_bf = sbt("b_dtr", [64, FC], BF16, stack=pB)
                dl32 = sbt("b_dl32", [128, FC], stack=pB)
                dacc = sbt("b_dacc", [128, 64], stack=pB)
                ps_c = psb("b_c", [128, FC], pB)
                ps_bc = psb("b_bc", [16, FC], pB)
                ps_dt = psb("b_dt", [64, FC], pB)
                hdl_prev = [None]
                hw_prev = [None]
                for tc in range(NT):
                    fsl = slice(tc * FC, (tc + 1) * FC)
                    ST.wait(sc)
                    for dk in range(NDIN):
                        s_ve.wait(sy)
                        LD.wait(ve)
                        if tc == 0:
                            hl = load(xiw[:, 3:FC + 3], spill_xi[dk * 128:(dk + 1) * 128, fsl])
                            LD.wait(ve, hl)
                            s_ve.inc(ve.tensor_copy(out=xiw[:, 0:3],
                                                    in_=halo_rv[:, dk * 3:dk * 3 + 3]))
                        else:
                            hl = load(xiw[:, 0:FC + 3],
                                      spill_xi[dk * 128:(dk + 1) * 128,
                                               tc * FC - 3:(tc + 1) * FC])
                            LD.wait(ve, hl)
                        ve.tensor_scalar(scr[0][:, 0:FC], xiw[:, 0:FC],
                                         cvw[:, dk * 4:dk * 4 + 1], None, ALU.mult)
                        for j in range(1, 4):
                            ve.scalar_tensor_tensor(scr[0][:, 0:FC], xiw[:, j:j + FC],
                                                    cvw[:, dk * 4 + j:dk * 4 + j + 1],
                                                    scr[0][:, 0:FC], ALU.mult, ALU.add)
                        s_ve.inc(ve.engine_nop()) if False else s_ve.inc(
                            ve.tensor_copy(out=scr[0][:, FC:2 * FC], in_=scr[0][:, 0:FC]))
                        s_ve.wait(sc)
                        s_sc.inc(sc.activation(out=xc_bf[dk][:], in_=scr[0][:, FC:2 * FC],
                                               func=AF.Silu, bias=cvb[:, dk:dk + 1]))
                        fence_sc()
                        s_sc.wait(g)
                        store(spill_xc[dk * 128:(dk + 1) * 128, fsl], xc_bf[dk][:])
                        s_sc.wait(ve)
                    gw_all(te)
                    for k in range(NDIN):
                        te.matmul(ps_dt[:], wx_bf[:, k * 96:k * 96 + 64], xc_bf[k][:],
                                  start=(k == 0), stop=(k == NDIN - 1))
                    for k in range(NDIN):
                        te.matmul(ps_bc[:], wx_bf[:, k * 96 + 64:k * 96 + 80], xc_bf[k][:],
                                  start=(k == 0), stop=(k == NDIN - 1))
                    for k in range(NDIN):
                        inst = te.matmul(ps_c[0:16, :], wx_bf[:, k * 96 + 80:k * 96 + 96],
                                         xc_bf[k][:], start=(k == 0), stop=(k == NDIN - 1))
                    s_te.inc(inst)
                    s_te.wait(sc)
                    s_sc.inc(sc.copy(out=dtr_bf[:], in_=ps_dt[:]))
                    s_sc.inc(sc.copy(out=Brow[:, fsl], in_=ps_bc[:]))
                    s_sc.inc(sc.copy(out=Crow[:, fsl], in_=ps_c[0:16, :]))
                    fence_sc()
                    s_sc.wait(te)
                    for dk in range(NDIN):
                        inst = te.matmul(ps_c[:], wdt_bf[:, dk * 128:(dk + 1) * 128],
                                         dtr_bf[:], start=True, stop=True)
                        s_te.inc(inst)
                        s_te.wait(sc)
                        s_te.wait(ve)
                        s_ve.wait(sc)
                        sc.activation(out=scr[0][:, 0:FC], in_=ps_c[:], func=AF.Abs,
                                      bias=bdt_sb[:, dk:dk + 1])
                        sc.activation(out=scr[0][:, 0:FC], in_=scr[0][:, 0:FC],
                                      func=AF.Exp, scale=-1.0)
                        s_sc.inc(sc.activation(out=scr[0][:, 0:FC], in_=scr[0][:, 0:FC],
                                               func=AF.Ln, bias=1.0))
                        s_sc.wait(ve)
                        ve.tensor_scalar(scr[0][:, FC:2 * FC], ps_c[:],
                                         bdt_sb[:, dk:dk + 1], 0.0, ALU.add, ALU.max)
                        if hdl_prev[0] is not None:
                            ST.wait(ve, hdl_prev[0])
                        s_ve.inc(ve.scalar_tensor_tensor(
                            dl32[:], scr[0][:, FC:2 * FC], 1.0, scr[0][:, 0:FC],
                            ALU.bypass, ALU.add,
                            accum_out=dacc[:, tc * 16 + dk:tc * 16 + dk + 1]))
                        fence_ve()
                        s_ve.wait(g)
                        hdl_prev[0] = store(spill_delta[dk * 128:(dk + 1) * 128, fsl],
                                            dl32[:])
                        if hw_prev[0] is not None:
                            ST.wait(ve, hw_prev[0])
                        s_ve.inc(ve.tensor_tensor(w_bf[:], dl32[:],
                                                  xc_bf[dk][:], ALU.mult))
                        fence_ve()
                        s_ve.wait(g)
                        hw_prev[0] = store(spill_w[dk * 128:(dk + 1) * 128, fsl],
                                           w_bf[:])
                        s_ve.wait(te)
                s_ve.wait(sc)
                s_sc.inc(sc.copy(out=dacc[:], in_=dacc[:]))
                fence_sc()
                s_sc.wait(ve)
                ve.tensor_tensor(dsum[:, 0:16], dacc[:, 0:16], dacc[:, 16:32], ALU.add)
                ve.tensor_tensor(dsum[:, 0:16], dsum[:, 0:16], dacc[:, 32:48], ALU.add)
                s_ve.inc(ve.tensor_tensor(dsum[:, 0:16], dsum[:, 0:16],
                                          dacc[:, 48:64], ALU.add))
                fence_ve()
                s_sc.wait(g)
                store(spill_bc[0:16, :], Brow[:])
                store(spill_bc[16:32, :], Crow[:])
                gw_all(sy)
                gw_all(g)
                gw_all(te)
                pB.close()

                acols = sbt("b_acols", [128, NDIN * DST], stack=px)
                hl = load(acols[:], alog_cols[:, :])
                LD.wait(sc, hl)
                s_sc.inc(sc.activation(out=acols[:], in_=acols[:], func=AF.Exp))
                s_sc.wait(ve)
                s_ve.inc(ve.tensor_scalar(acols[:], acols[:], -1.0, None, ALU.mult))
                fence_ve()
                P_t = sbt("b_P", [128, NDIN * DST], stack=px)
                S_t = sbt("b_S", [128, NDIN * DST], stack=px)
                s_ve.wait(sc)
                for dk in range(NDIN):
                    s_ve.inc(ve.tensor_scalar(P_t[:, dk * 16:(dk + 1) * 16],
                                              acols[:, dk * 16:(dk + 1) * 16],
                                              dsum[:, dk:dk + 1], None, ALU.mult))
                fence_ve()
                s_ve.wait(sc)
                s_sc.inc(sc.activation(out=P_t[:], in_=P_t[:], func=AF.Exp))
                fence_sc()

                dA = sbt("b_dA", [128, HF], stack=px)
                dBu = sbt("b_dBu", [128, HF], BF16, stack=px)
                hsb = sbt("b_hs", [128, HF], stack=px)
                hs2 = sbt("b_hs2", [128, HF], BF16, stack=px)
                dsl = sbt("b_dsl", [128, HF], stack=px)
                wslc = sbt("b_wslc", [128, HF], BF16, stack=px)
                Bb = sbt("b_Bb", [128, HF], BF16, stack=px)
                Cb = sbt("b_Cb", [128, HF], BF16, stack=px)
                Smid = sbt("b_Smid", [128, NDIN * DST], stack=px)
                prod = sbt("b_prod", [128, HF], BF16, stack=px)

                def scan_pass(phase, hstart_t):
                    for hh in range(2):
                        h0 = hh * HF
                        msl = c_mlast[:, TC - HF:TC]
                        for dk in range(NDIN):
                            s_sc.wait(sy)
                            s_ve.wait(sy)
                            hld = load(dsl[:], spill_delta[dk * 128:(dk + 1) * 128,
                                                           h0:h0 + HF])
                            hlw = load(wslc[:], spill_w[dk * 128:(dk + 1) * 128,
                                                        h0:h0 + HF])
                            LD.wait(sc, hld)
                            LD.wait(ve, hlw)
                            if phase == 2:
                                ps_y = psy_t[dk % 2]
                            for n in range(DST):
                                g_ = dk * 16 + n
                                hb = store(Bb[:], spill_bc[n:n + 1, h0:h0 + HF]
                                           .partition_broadcast(128))
                                if phase == 2:
                                    hc = store(Cb[:], spill_bc[16 + n:17 + n, h0:h0 + HF]
                                               .partition_broadcast(128))
                                s_sc.wait(ve)
                                s_sc.inc(sc.activation(out=dA[:], in_=dsl[:], func=AF.Exp,
                                                       scale=acols[:, g_:g_ + 1]))
                                ST.wait(ve, hb)
                                s_sc.wait(ve)
                                ve.tensor_tensor(dBu[:], wslc[:], Bb[:], ALU.mult)
                                if hh == 0:
                                    init = (hstart_t[:, g_:g_ + 1] if phase == 2 else 0.0)
                                else:
                                    init = Smid[:, g_:g_ + 1]
                                if phase == 1:
                                    ve.tensor_tensor_scan(hsb[:], dA[:], dBu[:], init,
                                                          ALU.mult, ALU.add)
                                    s_ve.inc(ve.scalar_tensor_tensor(
                                        scr[0][:, 0:HF], hsb[:], 1.0, msl,
                                        ALU.bypass, ALU.mult,
                                        accum_out=(Smid if hh == 0 else S_t)[:, g_:g_ + 1]))
                                else:
                                    ve.tensor_tensor_scan(hs2[:], dA[:], dBu[:], init,
                                                          ALU.mult, ALU.add)
                                    ve.scalar_tensor_tensor(
                                        scr[0][:, 0:HF], hs2[:], 1.0, msl,
                                        ALU.bypass, ALU.mult,
                                        accum_out=Smid[:, g_:g_ + 1])
                                    ST.wait(ve, hc)
                                    s_ve.inc(ve.tensor_tensor(prod[:], hs2[:], Cb[:],
                                                              ALU.mult))
                                    s_ve.wait(te)
                                    for fc2 in range(2):
                                        inst = te.matmul(
                                            ps_y[:, fc2 * FC:(fc2 + 1) * FC], c_idbf[:],
                                            prod[:, fc2 * FC:(fc2 + 1) * FC],
                                            start=(n == 0), stop=(n == DST - 1))
                                    s_te.inc(inst)
                                s_ve.wait(sc)
                                s_ve.wait(g)
                            if phase == 2:
                                s_te.wait(ve)
                                s_te.wait(sc)
                                s_ve.wait(sy)
                                hxc = load(wslc[:],
                                           spill_xc[dk * 128:(dk + 1) * 128, h0:h0 + HF])
                                hz = load(dBu[:], spill_z[dk * 128:(dk + 1) * 128,
                                                          h0:h0 + HF])
                                LD.wait(ve, hxc)
                                LD.wait(sc, hz)
                                s_ve.wait(sc)
                                s_sc.inc(sc.activation(out=Cb[:], in_=dBu[:],
                                                       func=AF.Silu))
                                fence_sc()
                                dvs = dv_sb[:, dk:dk + 1]
                                ve.scalar_tensor_tensor(scr[0][:, 0:HF], wslc[:], dvs,
                                                        ps_y[:], ALU.mult, ALU.add)
                                s_sc.wait(ve)
                                s_ve.inc(ve.tensor_tensor(gated[dk][:],
                                                          scr[0][:, 0:HF], Cb[:],
                                                          ALU.mult))
                                s_ve.wait(te)
                                s_ve.wait(g)
                        if phase == 1 and hh == 0:
                            s_ve.wait(sc)
                            s_sc.inc(sc.copy(out=Smid[:], in_=Smid[:]))
                            fence_sc()
                            s_sc.wait(ve)
                        if phase == 2 and hh == 0:
                            s_ve.wait(sc)
                            s_sc.inc(sc.copy(out=Smid[:], in_=Smid[:]))
                            fence_sc()
                            s_sc.wait(ve)
                        if phase == 2:
                            wout_phase(hh)

                scan_pass(1, None)
                s_ve.wait(sc)
                s_sc.inc(sc.copy(out=S_t[:], in_=S_t[:]))
                fence_sc()
                s_ve.wait(g)
                s_sc.wait(g)
                store(cc3_in[:, 0:256], S_t[:])
                store(cc3_in[:, 256:512], P_t[:])
                ST.wait(g)
                s_cc.inc(g.collective_compute(
                    "AllGather", ALU.bypass, replica_groups=[list(range(NCORES))],
                    ins=[cc3_in.opt()], outs=[cc3_out.opt()]))
                s_cc.wait(g)
                Hc3 = sbt("b_Hc", [128, NDIN * DST], stack=px)
                Hp3 = sbt("b_Hp", [128, NDIN * DST], stack=px)
                Sr3 = sbt("b_Sr", [128, 512], stack=px)
                hstart3 = sbt("b_hst", [128, NDIN * DST], stack=px)
                s_ve.wait(g)
                s_ve.inc(ve.memset(hstart3[:], 0.0))
                for b_ in range(2):
                    s_ve.inc(ve.memset(Hc3[:], 0.0))
                    fence_ve()
                    for k in range(4):
                        r = b_ * 4 + k
                        hr = store(Sr3[:], cc3_out[r * 128:(r + 1) * 128, :])
                        ST.wait(ve, hr)
                        j = r + 1
                        s_ve.inc(ve.scalar_tensor_tensor(
                            hstart3[:], Hc3[:], c_selq[:, r:r + 1], hstart3[:],
                            ALU.mult, ALU.add))
                        fence_ve()
                        if k < 3:
                            ve.tensor_tensor(Hp3[:], Hc3[:], Sr3[:, 256:512], ALU.mult)
                            fence_ve()
                            s_ve.inc(ve.tensor_tensor(Hc3[:], Hp3[:], Sr3[:, 0:256],
                                                      ALU.add))
                            fence_ve()
                        s_ve.wait(g)
                s_ve.wait(sc)
                s_sc.inc(sc.copy(out=hstart3[:], in_=hstart3[:]))
                fence_sc()
                s_sc.wait(ve)

                gated = [sbt(f"b_gt{k}", [128, HF], BF16, stack=px) for k in range(NDIN)]
                dv_sb = sbt("b_dv", [128, NDIN], stack=px)
                load(dv_sb[:], dvec_cols[:, :])
                psy_t = [psb("b_y0", [128, TC // 2]), psb("b_y1", [128, TC // 2])]
                wo_blk = sbt("b_wo", [128, D], BF16, stack=px)
                ps_wo = psb("b_wo_ps", [128, HF])

                def wout_phase(hh):
                    h0 = hh * HF
                    for po in range(NDBLK):
                        for k in range(NDIN):
                            hl = load(wo_blk[:, 0:128] if False else wo_blk[:, po * 128:(po + 1) * 128],
                                      cwout[k * 128:(k + 1) * 128,
                                            po * 128:(po + 1) * 128])
                            LD.wait(te, hl)
                            for fc2 in range(2):
                                inst = te.matmul(ps_wo[:, fc2 * FC:(fc2 + 1) * FC],
                                                 wo_blk[:, po * 128:(po + 1) * 128],
                                                 gated[k][:, fc2 * FC:(fc2 + 1) * FC],
                                                 start=(k == 0), stop=(k == NDIN - 1))
                            s_te.inc(inst)
                            s_te.wait(sy)
                        s_te.wait(ve)
                        s_ve.inc(ve.tensor_tensor(X[po][:, h0:h0 + HF],
                                                  X[po][:, h0:h0 + HF], ps_wo[:],
                                                  ALU.add))
                        s_ve.wait(te)

                scan_pass(2, hstart3)


        junk = sbt("junk", [1, 144])
        for ti_, ap_ in enumerate(all_inputs):
            w_ = min(ap_.shape[1], 4)
            load(junk[0:1, ti_ * 4:ti_ * 4 + w_], ap_[0:1, 0:w_])
        load(c_id32[:], ident32[:, :])
        load(c_ocol[:], ones_col[:, :])
        load(c_o128[:], ones128[:, :])
        load(c_iota[:], iota8[:, :])
        load(c_selq[:], selq[:, :])
        load(c_selp[:], selprev[:, :])
        load(c_mlast[:], mask_last[:, :])
        LD.wait(sc)
        s_sc.inc(sc.copy(out=c_idbf[:], in_=c_id32[:]))
        s_sc.inc(sc.copy(out=c_o128b[:], in_=c_o128[0:1, :]))
        for i in range(NDBLK):
            load(X[i][:], x_in[i * 128:(i + 1) * 128, :])
        maybe_dump("x0", [X[i][:] for i in range(NDBLK)])
        maybe_dump("c", [c_o128[:], c_id32[:], c_iota[:], c_selq[:]])

        if _DEBUG not in ("x0", "c"):
            emit_attn_a()
            if _UPTO >= 2:
                emit_moe(first=True)
            if _UPTO >= 3:
                emit_attn_b()
            if _UPTO >= 4:
                emit_moe(first=False)
        barrier()

        if not dumped[0]:
            for i in range(NDBLK):
                store(out[i * 128:(i + 1) * 128, :], X[i][:])
        ST.wait(g)

    return nc


_cached = {}


def _get_nc():
    if _UPTO not in _cached:
        _cached[_UPTO] = build_program()
    return _cached[_UPTO]


def kernel(**inp):
    f32 = np.float32
    x = np.asarray(inp["L0_attn_a_x"], f32)

    def C(a):
        return np.ascontiguousarray(a)

    common = {
        "bwT": C(np.asarray(inp["L0_attn_a_B_w"], f32).T),
        "cwT": C(np.asarray(inp["L0_attn_a_C_w"], f32).T),
        "w1T": C(np.asarray(inp["L0_attn_a_w1"], f32).T),
        "w3T": C(np.asarray(inp["L0_attn_a_w3"], f32).T),
        "w2T": C(np.asarray(inp["L0_attn_a_w2"], f32).T),
        "ll0": C(np.asarray(inp["L0_attn_a_log_lambda"], f32)[0:128, None]),
        "ll1": C(np.asarray(inp["L0_attn_a_log_lambda"], f32)[128:256, None]),
        "ssmw_cols": C(np.asarray(inp["L0_attn_a_ssm_norm_w"], f32).reshape(NDBLK, 128).T),
        "ffnw_cols": C(np.asarray(inp["L0_attn_a_ffn_norm_w"], f32).reshape(NDBLK, 128).T),
        "dskip_cols": C(np.asarray(inp["L0_attn_a_D_skip"], f32).reshape(NDBLK, 128).T),
        "gwT_cols": C(np.asarray(inp["L0_ffn_gate_weight"], f32).T.reshape(NDBLK, 128, E)
                      .transpose(1, 0, 2).reshape(128, NDBLK * E)),
        "wgT": C(np.asarray(inp["L0_ffn_w_gate"], f32).transpose(0, 2, 1).reshape(E * D, EI)),
        "wuT": C(np.asarray(inp["L0_ffn_w_up"], f32).transpose(0, 2, 1).reshape(E * D, EI)),
        "wdT": C(np.asarray(inp["L0_ffn_w_down"], f32).transpose(0, 2, 1).reshape(E * EI, D)),
        "wsgT": C(np.asarray(inp["L0_ffn_ws_gate"], f32).T),
        "wsuT": C(np.asarray(inp["L0_ffn_ws_up"], f32).T),
        "wsdT": C(np.asarray(inp["L0_ffn_ws_down"], f32).T),
        "winT": C(np.asarray(inp["L0_attn_b_W_in"], f32)),
        "convw_cols": C(np.asarray(inp["L0_attn_b_conv_w"], f32).reshape(NDIN, 128, DCONV)
                        .transpose(1, 0, 2).reshape(128, NDIN * DCONV)),
        "convb_cols": C(np.asarray(inp["L0_attn_b_conv_b"], f32).reshape(NDIN, 128).T),
        "wxT": C(np.asarray(inp["L0_attn_b_W_x"], f32)),
        "wdtT": C(np.asarray(inp["L0_attn_b_W_dt"], f32)),
        "bdt_cols": C(np.asarray(inp["L0_attn_b_b_dt"], f32).reshape(NDIN, 128).T),
        "alog_cols": C(np.asarray(inp["L0_attn_b_A_log"], f32).reshape(NDIN, 128, DST)
                       .transpose(1, 0, 2).reshape(128, NDIN * DST)),
        "dvec_cols": C(np.asarray(inp["L0_attn_b_D"], f32).reshape(NDIN, 128).T),
        "woutT": C(np.asarray(inp["L0_attn_b_W_out"], f32)),
        "ident32": np.eye(128, dtype=f32),
        "ones_col": np.ones((128, 1), f32),
        "ones_row": np.ones((1, FC), f32),
        "ones128": np.ones((128, 128), f32),
        "iota8": np.tile(np.arange(E, dtype=f32), (128, 1)),
        "mask_last": np.concatenate([np.zeros((128, TC - 1), f32),
                                     np.ones((128, 1), f32)], axis=1),
    }

    in_maps = []
    for c in range(NCORES):
        b_, q = c // 4, c % 4
        m = dict(common)
        m["x"] = C(x[b_, q * TC:(q + 1) * TC, :].T)
        sq = np.zeros((128, NCORES), f32)
        sq[:, c] = 1.0
        m["selq"] = sq
        sp = np.zeros((128, NCORES), f32)
        if q > 0:
            sp[:, c - 1] = 1.0
        m["selprev"] = sp
        in_maps.append(m)

    nc = _get_nc()
    res = run_bass_kernel_spmd(nc, in_maps, core_ids=list(range(NCORES)))
    outp = np.empty((B, T, D), f32)
    for c in range(NCORES):
        b_, q = c // 4, c % 4
        outp[b_, q * TC:(q + 1) * TC, :] = res.results[c]["out"].T
    return outp
